# version 12 (final)
# baseline (speedup 1.0000x reference)
import contextlib
import numpy as np
import concourse.bass as bass
import concourse.mybir as mybir
from concourse.bass_utils import run_bass_kernel_spmd

F32 = mybir.dt.float32
BF16 = mybir.dt.bfloat16
AF = mybir.ActivationFunctionType
ALU = mybir.AluOpType
AX = mybir.AxisListType

H = 2048
KC = H // 128
TC = 512
NCH = 8
NT = TC // 128
NE = 16
FE = 1024
BIG = 1e30
EPS = 1e-5

ENGS = ("tensor", "vector", "scalar", "sync", "gpsimd")


class Sem:
    def __init__(self, h):
        self.h = h
        self.n = 0


class B:

    def __init__(self, nc):
        self.nc = nc
        self.prog = {e: [] for e in ENGS}
        self.wc = {e: {} for e in ENGS}
        self.all_waits = []
        self.auto = {}

    def wait(self, eng, mark):
        if mark is None:
            return
        sem, cnt = mark
        key = id(sem)
        if self.wc[eng].get(key, 0) >= cnt:
            return
        self.wc[eng][key] = cnt
        self.all_waits.append((sem, cnt))

        def th(e, s=sem, c=cnt):
            e.wait_ge(s.h, c)

        self.prog[eng].append(th)

    def waits(self, eng, marks):
        best = {}
        for m in marks:
            if m is None:
                continue
            sem, cnt = m
            k = id(sem)
            if k not in best or best[k][1] < cnt:
                best[k] = m
        for m in best.values():
            self.wait(eng, m)

    def emit(self, eng, fn, sem=None, n=1):
        a = self.auto.get(eng)
        if a is not None:
            self.wait(eng, (a, a.n))
            if sem is None:
                sem, n = a, 1
        mark = None
        if sem is not None:
            sem.n += n
            mark = (sem, sem.n)

        def th(e, fn=fn, sem=sem, n=n):
            ins = fn(e)
            if sem is not None:
                ins.then_inc(sem.h, n)

        self.prog[eng].append(th)
        return mark

    def flush(self, block):
        for eng in ENGS:
            prog = self.prog[eng]
            deco = getattr(block, eng)

            def body(e, prog=prog):
                for th in prog:
                    th(e)

            deco(body)


def build_core(debug=False):
    nc = bass.Bass("TRN2", target_bir_lowering=False)

    x_d = nc.dram_tensor("x", [NCH * TC, H], F32, kind="ExternalInput").ap()
    wi_d = nc.dram_tensor("wi_t", [H, H], F32, kind="ExternalInput").ap()
    wf_d = nc.dram_tensor("wf_t", [H, H], F32, kind="ExternalInput").ap()
    wg_d = nc.dram_tensor("wg_t", [H, H], F32, kind="ExternalInput").ap()
    wo_d = nc.dram_tensor("wo_t", [H, H], F32, kind="ExternalInput").ap()
    wr_d = nc.dram_tensor("wr_t", [H, NE], F32, kind="ExternalInput").ap()
    gn_d = nc.dram_tensor("gnorm_fm", [128, KC], F32, kind="ExternalInput").ap()
    bias_d = nc.dram_tensor("bias_r", [NE], F32, kind="ExternalInput").ap()
    gw_d = nc.dram_tensor("gw_t", [NE, H, FE], F32, kind="ExternalInput").ap()
    uw_d = nc.dram_tensor("uw_t", [NE, H, FE], F32, kind="ExternalInput").ap()
    dw_d = nc.dram_tensor("dw_t", [NE, FE, H], F32, kind="ExternalInput").ap()
    shg_d = nc.dram_tensor("shg_t", [H, H], F32, kind="ExternalInput").ap()
    shu_d = nc.dram_tensor("shu_t", [H, H], F32, kind="ExternalInput").ap()
    shd_d = nc.dram_tensor("shd_t", [H, H], F32, kind="ExternalInput").ap()
    out_d = nc.dram_tensor("out", [NCH * TC, H], F32, kind="ExternalOutput").ap()
    if debug:
        dbg_xn = nc.dram_tensor("dbg_xn", [128, KC * TC], F32, kind="ExternalOutput").ap()
        dbg_sc = nc.dram_tensor("dbg_sc", [128, NT * NE], F32, kind="ExternalOutput").ap()
        dbg_wx = nc.dram_tensor("dbg_wx", [128, NT * NE], F32, kind="ExternalOutput").ap()
        dbg_o = nc.dram_tensor("dbg_o", [128, KC * TC], F32, kind="ExternalOutput").ap()
        dbg_og = nc.dram_tensor("dbg_og", [128, KC * TC], F32, kind="ExternalOutput").ap()
        dbg_rsq = nc.dram_tensor("dbg_rsq", [1, 512], F32, kind="ExternalOutput").ap()
        dbg_oa1 = nc.dram_tensor("dbg_oa1", [128, NT * H], F32, kind="ExternalOutput").ap()
        dbg_oa2 = nc.dram_tensor("dbg_oa2", [128, NT * H], F32, kind="ExternalOutput").ap()
        dbg_act = nc.dram_tensor("dbg_act", [128, KC * TC], F32, kind="ExternalOutput").ap()

    def pan2(ap, c0, cols, kc):
        return ap[:, c0:c0 + cols].rearrange("(c p) f -> p c f", p=128)

    panels = []
    P_WIF = len(panels)
    for q in range(4):
        panels.append(pan2(wi_d, q * 512, 512, KC))
        panels.append(pan2(wf_d, q * 512, 512, KC))
    P_WG = len(panels)
    for q in range(4):
        panels.append(pan2(wg_d, q * 512, 512, KC))
    P_SHG = len(panels)
    for q in range(4):
        panels.append(pan2(shg_d, q * 512, 512, KC))
        panels.append(pan2(shu_d, q * 512, 512, KC))
    P_WO = len(panels)
    for q in range(4):
        panels.append(pan2(wo_d, q * 512, 512, KC))
    P_SHD = len(panels)
    for q in range(4):
        panels.append(pan2(shd_d, q * 512, 512, KC))
    P_GU = {}
    P_D = {}
    for n in range(NE):
        P_GU[n] = len(panels)
        for h2 in range(2):
            panels.append(pan2(gw_d[n], h2 * 512, 512, KC))
            panels.append(pan2(uw_d[n], h2 * 512, 512, KC))
        if n >= 1:
            P_D[n - 1] = len(panels)
            for dp in range(2):
                panels.append(pan2(dw_d[n - 1], dp * 1024, 1024, 8))
    P_D[NE - 1] = len(panels)
    for dp in range(2):
        panels.append(pan2(dw_d[NE - 1], dp * 1024, 1024, 8))
    NPAN = len(panels)
    KCN = [KC] * NPAN
    for n in range(NE):
        KCN[P_D[n]] = 8
        KCN[P_D[n] + 1] = 8

    wc_d = nc.dram_tensor("wcache", [NPAN, 128, 8192], BF16, kind="Internal").ap()

    with contextlib.ExitStack() as st:
        E = st.enter_context

        xn_sb = E(nc.sbuf_tensor("xn_sb", [128, 2 * KC * TC], BF16))
        o_sb = E(nc.sbuf_tensor("o_sb", [128, KC * TC], BF16))
        oa_sb = E(nc.sbuf_tensor("oa_sb", [128, NT * H], F32))
        act_sb = E(nc.sbuf_tensor("act_sb", [128, KC * TC], BF16))
        slots = [E(nc.sbuf_tensor("slot%d" % i, [128, 8192], BF16)) for i in range(4)]
        xln = E(nc.sbuf_tensor("xln", [128, H], F32))
        xbf = E(nc.sbuf_tensor("xbf", [128, H], BF16))
        xtok = [E(nc.sbuf_tensor("xtok%d" % i, [128, H], F32)) for i in range(2)]
        tA = [E(nc.sbuf_tensor("tA%d" % i, [128, 512], BF16)) for i in range(2)]
        tB = [E(nc.sbuf_tensor("tB%d" % i, [128, 512], BF16)) for i in range(2)]
        tC = [E(nc.sbuf_tensor("tC%d" % i, [128, 512], BF16)) for i in range(2)]
        tD = [E(nc.sbuf_tensor("tD%d" % i, [128, 512], BF16)) for i in range(2)]
        tE32 = E(nc.sbuf_tensor("tE32", [128, 512], F32))
        sg_sb = [E(nc.sbuf_tensor("sg%d" % i, [128, 512], BF16)) for i in range(2)]
        sqt_sb = [E(nc.sbuf_tensor("sqt%d" % i, [128, 512], BF16)) for i in range(2)]
        carry_sb = E(nc.sbuf_tensor("carry_sb", [128, KC], F32))
        scores_sb = E(nc.sbuf_tensor("scores_sb", [128, 2 * NT * NE], F32))
        wexp_sb = E(nc.sbuf_tensor("wexp_sb", [128, 2 * NT * NE], F32))
        rt_sb = E(nc.sbuf_tensor("rt_sb", [128, 176], F32))
        st_sb = E(nc.sbuf_tensor("st_sb", [128, 16], F32))
        gn_sb = E(nc.sbuf_tensor("gn_sb", [128, KC], F32))
        bb_sb = E(nc.sbuf_tensor("bb_sb", [128, NE], F32))
        wr_sb = E(nc.sbuf_tensor("wr_sb", [128, KC * NE], BF16))
        idn_sb = E(nc.sbuf_tensor("idn_sb", [128, 128], F32))
        onec_sb = E(nc.sbuf_tensor("onec_sb", [128, 1], BF16))
        oner_sb = E(nc.sbuf_tensor("oner_sb", [1, 128], F32))
        rsq_sb = E(nc.sbuf_tensor("rsq_sb", [1, 512], F32))
        eps_sb = E(nc.sbuf_tensor("eps_sb", [128, 1], F32))
        brow_sb = E(nc.sbuf_tensor("brow_sb", [1, NE], F32))

        fm = [E(nc.psum_tensor("fm%d" % i, [128, 512], F32)) for i in range(4)]
        pt = [E(nc.psum_tensor("pt%d" % i, [128, 1024], F32)) for i in range(2)]

        SP = Sem(E(nc.semaphore("SP")))
        SV = Sem(E(nc.semaphore("SV")))
        SS = Sem(E(nc.semaphore("SS")))
        SG = Sem(E(nc.semaphore("SG")))
        SLg = [Sem(E(nc.semaphore("SLg%d" % i))) for i in range(4)]
        SLs = [Sem(E(nc.semaphore("SLs%d" % i))) for i in range(4)]
        SW = [Sem(E(nc.semaphore("SW%d" % i))) for i in range(4)]
        SWR = Sem(E(nc.semaphore("SWR")))
        SX = Sem(E(nc.semaphore("SX")))
        SXA = Sem(E(nc.semaphore("SXA")))
        SXT = Sem(E(nc.semaphore("SXT")))
        SOUT = Sem(E(nc.semaphore("SOUT")))
        SPR = Sem(E(nc.semaphore("SPR")))
        SDB = Sem(E(nc.semaphore("SDB")))

        K = B(nc)
        K.auto = {"vector": SV, "scalar": SS}

        slot_st = [{"load": None, "readers": []} for _ in range(4)]
        bank_st = {nm: {"write": None, "readers": []}
                   for nm in ("fm0", "fm1", "fm2", "fm3", "pt0", "pt1")}
        res_mark = {}

        def bank_write(nm, extra=()):
            s = bank_st[nm]
            K.waits("tensor", s["readers"] + list(extra))
            s["readers"] = []

        def bank_read(eng, nm):
            K.wait(eng, bank_st[nm]["write"])

        def bank_read_done(nm, mark):
            bank_st[nm]["readers"].append(mark)

        cursor = [0]

        def load_next_panel():
            g = cursor[0]
            if g >= NCH * NPAN:
                return
            cursor[0] += 1
            c, p = divmod(g, NPAN)
            sl = p % 4
            stt = slot_st[sl]
            fp32 = c == 0 or (c == 1 and p % 2 == 1)
            eng = "gpsimd" if (fp32 or p % 2 == 1) else "sync"
            sem = SLg[sl] if eng == "gpsimd" else SLs[sl]
            K.waits(eng, stt["readers"])
            K.wait(eng, (SLg[sl], SLg[sl].n))
            K.wait(eng, (SLs[sl], SLs[sl].n))
            if fp32:
                dst = slots[sl][:].rearrange("p (c f) -> p c f", c=KCN[p])
                lm = K.emit(eng, lambda e, d=dst, s=panels[p]: e.dma_start(d, s), sem, 16)
            else:
                lm = K.emit(eng, lambda e, d=slots[sl][:], s=wc_d[p]: e.dma_start(d, s), sem, 16)
            stt["load"] = lm
            stt["readers"] = []
            if fp32 and (c == 0) == (p % 2 == 0):
                K.wait("sync", lm)
                K.wait("sync", (SW[sl], SW[sl].n))
                wm = K.emit("sync", lambda e, d=wc_d[p], s=slots[sl][:]: e.dma_start(d, s), SW[sl], 16)
                stt["readers"].append(wm)

        def panel_slot(p):
            sl = p % 4
            K.wait("tensor", slot_st[sl]["load"])
            return slots[sl]

        def panel_done(p, mark):
            slot_st[p % 4]["readers"].append(mark)
            load_next_panel()

        tr_bank = [0]
        prologue_marks = {}

        def head_dma(c, tt):
            row0 = c * TC
            deng = "sync" if c == 0 else "scalar"
            xt_in = xtok[tt % 2]
            K.waits(deng, res_mark.get("xin_rd%d" % (tt % 2), []))
            K.wait(deng, (SX, SX.n))
            m_x = K.emit(deng, lambda e, d=xt_in[:],
                         s=x_d[row0 + tt * 128: row0 + (tt + 1) * 128, :]:
                         e.dma_start(d, s), SX, 16)
            res_mark["head_x%d" % (tt % 2)] = m_x

        def head_ln(c, tt):
            m_eps = prologue_marks["eps"]
            m_x = res_mark["head_x%d" % (tt % 2)]
            xt_in = xtok[tt % 2]
            sb = 8 * (tt % 2)
            rs = st_sb[:, sb + 0:sb + 1]; qs = st_sb[:, sb + 1:sb + 2]
            mmn = st_sb[:, sb + 2:sb + 3]; msq = st_sb[:, sb + 3:sb + 4]
            var = st_sb[:, sb + 4:sb + 5]; stdv = st_sb[:, sb + 5:sb + 6]
            rstd = st_sb[:, sb + 6:sb + 7]; negb = st_sb[:, sb + 7:sb + 8]
            K.wait("vector", m_x)
            K.waits("vector", res_mark.get("st_rd%d" % (tt % 2), []))
            m_rs = K.emit("vector", lambda e, o=rs, i=xt_in[:]: e.reduce_sum(o, i, AX.X))
            K.wait("scalar", m_x)
            K.waits("scalar", res_mark.get("qs_rd%d" % (tt % 2), []))
            m_sq = K.emit("scalar", lambda e, o=xln[:], i=xt_in[:], a=qs:
                          e.activation(o, i, AF.Square, accum_out=a))
            K.wait("vector", m_sq)
            K.emit("vector", lambda e, o=mmn, i=rs: e.tensor_scalar_mul(o, i, 1.0 / H))
            K.emit("vector", lambda e, o=msq, a=mmn: e.tensor_mul(o, a, a))
            K.emit("vector", lambda e, o=var, i=qs: e.tensor_scalar_mul(o, i, 1.0 / H))
            m_v = K.emit("vector", lambda e, o=var, a=var, b=msq: e.tensor_sub(o, a, b))
            res_mark["qs_rd%d" % (tt % 2)] = [m_v]
            K.wait("scalar", m_v)
            K.wait("scalar", m_eps)
            m_std = K.emit("scalar", lambda e, o=stdv, i=var:
                           e.activation(o, i, AF.Sqrt, bias=eps_sb[:]))
            K.wait("vector", m_std)
            K.emit("vector", lambda e, o=rstd, i=stdv: e.reciprocal(o, i))
            K.emit("vector", lambda e, o=negb, a=mmn, b=rstd: e.tensor_mul(o, a, b))
            m_nb = K.emit("vector", lambda e, o=negb, i=negb: e.tensor_scalar_mul(o, i, -1.0))
            K.wait("scalar", m_nb)
            K.waits("scalar", res_mark.get("xbf_rd", []))
            m_app = K.emit("scalar", lambda e, o=xbf[:], i=xt_in[:], sc=rstd, b=negb:
                           e.activation(o, i, AF.Identity, bias=b, scale=sc))
            res_mark["st_rd%d" % (tt % 2)] = [m_app]
            res_mark["xin_rd%d" % (tt % 2)] = [m_rs, m_app]
            K.waits("scalar", res_mark.get("xn_rd%d" % (c % 2), []))
            K.wait("scalar", (SXT, SXT.n))
            xn5 = xn_sb[:].rearrange("p (pr hc tl t) -> p pr hc tl t", pr=2, hc=KC, t=128)
            m_xt = K.emit("scalar", lambda e, d=xn5[:, c % 2, :, tt, :], s=xbf[:]:
                          e.dma_start(d, s, transpose=True), SXT, 16)
            res_mark["xbf_rd"] = [m_xt]
            res_mark["head_xt"] = m_xt
            if tt == NT - 1:
                res_mark["xn_all%d" % (c % 2)] = m_xt

        def head_tr(c, tt):
            par = c % 2
            xnb = par * KC * TC
            m_wr = prologue_marks["wr"]
            m_bbc = prologue_marks["bbc"]
            m_xt = res_mark["head_xt"]
            bk = tr_bank[0] % 4
            tr_bank[0] += 1
            nm = "fm%d" % bk
            bank_write(nm)
            K.wait("tensor", m_wr)
            K.wait("tensor", m_xt)
            for kc in range(KC):
                is_l = kc == KC - 1
                mr = K.emit("tensor", lambda e, o=fm[bk][:, 0:NE],
                            l=xn_sb[:, xnb + kc * TC + tt * 128: xnb + kc * TC + (tt + 1) * 128],
                            r=wr_sb[:, kc * NE:(kc + 1) * NE], s=(kc == 0), p=is_l:
                            e.matmul(o, l, r, start=s, stop=p), SP if is_l else None, 1)
            bank_st[nm]["write"] = mr
            bank_read("scalar", nm)
            m_sc = K.emit("scalar", lambda e,
                          o=scores_sb[:, par * 64 + tt * NE: par * 64 + (tt + 1) * NE],
                          i=fm[bk][:, 0:NE]: e.activation(o, i, AF.Sigmoid))
            bank_read_done(nm, m_sc)
            K.wait("vector", m_sc)
            K.wait("vector", m_bbc)
            S_ = scores_sb[:, par * 64 + tt * NE: par * 64 + (tt + 1) * NE]
            r1 = rt_sb[:, 0:16]; r3 = rt_sb[:, 16:32]; r4 = rt_sb[:, 32:48]
            r8 = rt_sb[:, 48:64]; r9 = rt_sb[:, 64:80]; r10 = rt_sb[:, 80:96]
            r11 = rt_sb[:, 96:112]; r12 = rt_sb[:, 112:128]; r13 = rt_sb[:, 128:144]
            g4 = rt_sb[:, 144:148]; g4b = rt_sb[:, 148:152]; gsum = rt_sb[:, 152:156]
            gm = rt_sb[:, 156:160]; e1 = rt_sb[:, 160:164]; e2 = rt_sb[:, 164:168]
            sc5 = rt_sb[:, 168:176]
            V = "vector"
            K.emit(V, lambda e, S=S_: e.tensor_add(r1, S, bb_sb[:]))
            r1v = r1.rearrange("p (g e) -> p g e", e=4)
            r3v = r3.rearrange("p (g e) -> p g e", e=4)
            r4v = r4.rearrange("p (g e) -> p g e", e=4)
            K.emit(V, lambda e: e.reduce_max(g4, r1v, AX.X))
            K.emit(V, lambda e: e.tensor_tensor(r3v, r1v, g4.unsqueeze(2).broadcast_to([128, 4, 4]),
                                                ALU.is_equal))
            K.emit(V, lambda e: e.scalar_tensor_tensor(r4v, r3v, -BIG, r1v, ALU.mult, ALU.add))
            K.emit(V, lambda e: e.reduce_max(g4b, r4v, AX.X))
            K.emit(V, lambda e: e.tensor_add(gsum, g4, g4b))
            K.emit(V, lambda e: e.reduce_max(gm[:, 0:1], gsum, AX.X))
            K.emit(V, lambda e: e.tensor_scalar(e1, gsum, gm[:, 0:1], None, ALU.is_equal))
            K.emit(V, lambda e: e.scalar_tensor_tensor(gsum, e1, -BIG, gsum, ALU.mult, ALU.add))
            K.emit(V, lambda e: e.reduce_max(gm[:, 1:2], gsum, AX.X))
            K.emit(V, lambda e: e.tensor_scalar(e2, gsum, gm[:, 1:2], None, ALU.is_equal))
            K.emit(V, lambda e: e.tensor_add(e1, e1, e2))
            r8v = r8.rearrange("p (g e) -> p g e", e=4)
            K.emit(V, lambda e: e.tensor_copy(r8v, e1.unsqueeze(2).broadcast_to([128, 4, 4])))
            K.emit(V, lambda e: e.tensor_scalar(r9, r8, BIG, -BIG, ALU.mult, ALU.add))
            K.emit(V, lambda e: e.tensor_add(r10, r1, r9))
            K.emit(V, lambda e: e.tensor_copy(r11, r10))
            for _r in range(4):
                K.emit(V, lambda e: e.reduce_max(sc5[:, 0:1], r10, AX.X))
                K.emit(V, lambda e: e.tensor_scalar(r12, r10, sc5[:, 0:1], None, ALU.is_equal))
                K.emit(V, lambda e: e.scalar_tensor_tensor(r10, r12, -BIG, r10, ALU.mult, ALU.add))
            K.emit(V, lambda e: e.tensor_tensor(r12, r10, r11, ALU.not_equal))
            K.emit(V, lambda e, S=S_: e.tensor_mul(r13, r12, S))
            K.emit(V, lambda e: e.reduce_sum(sc5[:, 1:2], r13, AX.X))
            K.emit(V, lambda e: e.tensor_scalar_add(sc5[:, 2:3], sc5[:, 1:2], 1e-20))
            K.emit(V, lambda e: e.reciprocal(sc5[:, 3:4], sc5[:, 2:3]))
            K.emit(V, lambda e, o=wexp_sb[:, par * 64 + tt * NE: par * 64 + (tt + 1) * NE]:
                   e.tensor_scalar(o, r13, sc5[:, 3:4], 2.5, ALU.mult, ALU.mult))

        def emit_head(c):
            head_dma(c, 0)
            head_dma(c, 1)
            for tt in range(NT):
                head_ln(c, tt)
                head_tr(c, tt)
                if tt + 2 < NT:
                    head_dma(c, tt + 2)

        iof_v = rt_sb[:, 0:128]
        iop_v = rt_sb[:, 128:129]
        K.emit("gpsimd", lambda e: e.iota(iof_v, [[1, 128]], channel_multiplier=0,
                                          allow_small_or_imprecise_dtypes=True), SG, 1)
        m_iof = K.emit("gpsimd", lambda e: e.iota(iop_v, [[1, 1]], channel_multiplier=1,
                                                  allow_small_or_imprecise_dtypes=True), SG, 1)
        K.wait("vector", m_iof)
        m_idn = K.emit("vector", lambda e: e.tensor_scalar(idn_sb[:], iof_v, iop_v,
                                                           None, ALU.is_equal))
        K.emit("vector", lambda e: e.memset(onec_sb[:], 1.0))
        m_oner = K.emit("vector", lambda e: e.memset(oner_sb[:], 1.0))
        m_carry0 = K.emit("vector", lambda e: e.memset(carry_sb[:], 0.0))
        m_eps = K.emit("vector", lambda e: e.memset(eps_sb[:], EPS))
        K.wait("gpsimd", (SWR, SWR.n))
        m_wr = K.emit("gpsimd", lambda e: e.dma_start(
            wr_sb[:].rearrange("p (c f) -> p c f", c=KC),
            wr_d.rearrange("(c p) f -> p c f", p=128)), SWR, 16)
        K.wait("sync", (SPR, SPR.n))
        m_gn = K.emit("sync", lambda e: e.dma_start(gn_sb[:], gn_d), SPR, 16)
        K.wait("sync", (SPR, SPR.n))
        m_br = K.emit("sync", lambda e: e.dma_start(brow_sb[:], bias_d.unsqueeze(0)), SPR, 16)
        K.wait("tensor", m_br)
        K.wait("tensor", m_oner)
        bank_write("fm0")
        m_bb = K.emit("tensor", lambda e: e.matmul(fm[0][:, 0:NE], oner_sb[:], brow_sb[:],
                                                   start=True, stop=True), SP, 1)
        bank_st["fm0"]["write"] = m_bb
        bank_read("vector", "fm0")
        m_bbc = K.emit("vector", lambda e: e.tensor_copy(bb_sb[:], fm[0][:, 0:NE]))
        bank_read_done("fm0", m_bbc)
        prologue_marks.update(idn=m_idn, wr=m_wr, bbc=m_bbc, eps=m_eps)

        emit_head(0)
        for _ in range(4):
            load_next_panel()
        K.wait("scalar", (SXA, SXA.n))
        m_oax = K.emit("scalar", lambda e, d=oa_sb[:].rearrange("p (n f) -> p n f", n=NT),
                       s=x_d[0:TC, :].rearrange("(n p) f -> p n f", p=128):
                       e.dma_start(d, s), SXA, 16)
        res_mark["oa_x"] = m_oax

        for c in range(NCH):
            par = c % 2
            xnb = par * KC * TC
            row0 = c * TC
            xn_all = res_mark["xn_all%d" % par]
            oa_reads = []
            ptn = [0]

            def dump(dst, srcbuf, wait_marks):
                K.waits("gpsimd", wait_marks)
                K.wait("gpsimd", (SDB, SDB.n))
                return K.emit("gpsimd", lambda e, d=dst, s=srcbuf: e.dma_start(d, s), SDB, 16)

            if debug and c == 0:
                dbg_gate = [dump(dbg_xn, xn_sb[:, 0:KC * TC], [xn_all]),
                            dump(dbg_sc, scores_sb[:, 0:64], [(SS, SS.n)]),
                            dump(dbg_wx, wexp_sb[:, 0:64], [(SV, SV.n)])]
            else:
                dbg_gate = []

            prev_wo = res_mark.get("wo_done")

            def sq_sq(d):
                rr = d % 2
                K.wait("scalar", res_mark["scan%d" % d])
                K.waits("scalar", res_mark.get("sqt_rd%d" % rr, []))
                msq_ = K.emit("scalar", lambda e, o=sqt_sb[rr][:], i=o_sb[:, d * TC:(d + 1) * TC]:
                              e.activation(o, i, AF.Square))
                res_mark["msq%d" % d] = msq_

            def sq_mm(d):
                rr = d % 2
                K.wait("tensor", res_mark["msq%d" % d])
                K.wait("tensor", m_carry0)
                if d == 0:
                    bank_write("pt0")
                mon = K.emit("tensor", lambda e, o=pt[0][0:1, 0:512], l=onec_sb[:], r=sqt_sb[rr][:],
                             s=(d == 0), p=(d == KC - 1): e.matmul(o, l, r, start=s, stop=p), SP, 1)
                res_mark["sqt_rd%d" % rr] = [mon]
                return mon

            for q in range(4):
                p_wi = P_WIF + 2 * q
                p_wf = P_WIF + 2 * q + 1
                sl_i = panel_slot(p_wi)
                sl_f = panel_slot(p_wf)
                for dt2 in range(4):
                    d = q * 4 + dt2
                    bii = 2 * (dt2 % 2)
                    bfi = bii + 1
                    bi = "fm%d" % bii
                    bf_ = "fm%d" % bfi
                    if d >= 3:
                        sq_sq(d - 3)
                    K.wait("tensor", xn_all)
                    bank_write(bi)
                    for kc in range(KC):
                        last = kc == KC - 1
                        mi = K.emit("tensor", lambda e, o=fm[bii][:],
                                    l=sl_i[:, kc * 512 + dt2 * 128: kc * 512 + (dt2 + 1) * 128],
                                    r=xn_sb[:, xnb + kc * TC: xnb + (kc + 1) * TC], s=(kc == 0), p=last:
                                    e.matmul(o, l, r, start=s, stop=p), SP if last else None, 1)
                    bank_st[bi]["write"] = mi
                    if d >= 3:
                        sq_mm(d - 3)
                    bank_write(bf_)
                    for kc in range(KC):
                        last = kc == KC - 1
                        mf = K.emit("tensor", lambda e, o=fm[bfi][:],
                                    l=sl_f[:, kc * 512 + dt2 * 128: kc * 512 + (dt2 + 1) * 128],
                                    r=xn_sb[:, xnb + kc * TC: xnb + (kc + 1) * TC], s=(kc == 0), p=last:
                                    e.matmul(o, l, r, start=s, stop=p), SP if last else None, 1)
                    bank_st[bf_]["write"] = mf
                    if dt2 == 3:
                        panel_done(p_wi, mi)
                        panel_done(p_wf, mf)
                    rr = d % 2
                    K.waits("scalar", res_mark.get("ifr%d" % rr, []))
                    bank_read("scalar", bi)
                    ma = K.emit("scalar", lambda e, o=tA[rr][:], i=fm[bii][:]:
                                e.activation(o, i, AF.Sigmoid))
                    bank_read("scalar", bf_)
                    mb = K.emit("scalar", lambda e, o=tB[rr][:], i=fm[bfi][:]:
                                e.activation(o, i, AF.Sigmoid, scale=-1.0))
                    mc = K.emit("scalar", lambda e, o=tC[rr][:], i=fm[bfi][:]:
                                e.activation(o, i, AF.Sigmoid))
                    bank_read_done(bf_, mc)
                    K.wait("vector", ma)
                    bank_read("vector", bi)
                    m1 = K.emit("vector", lambda e, o=tD[rr][:], a=fm[bii][:], b=tA[rr][:]:
                                e.tensor_mul(o, a, b))
                    bank_read_done(bi, m1)
                    K.wait("vector", mb)
                    m2 = K.emit("vector", lambda e, o=tD[rr][:], a=tD[rr][:], b=tB[rr][:]:
                                e.tensor_mul(o, a, b))
                    K.wait("vector", mc)
                    if prev_wo is not None:
                        K.wait("vector", prev_wo)
                    if debug:
                        K.waits("vector", res_mark.get("dbg_o_gate", []))
                    m3 = K.emit("vector", lambda e, o=o_sb[:, d * TC:(d + 1) * TC], g=tC[rr][:],
                                v=tD[rr][:], ini=carry_sb[:, d:d + 1]:
                                e.tensor_tensor_scan(o, g, v, ini, ALU.mult, ALU.add))
                    K.emit("vector", lambda e, o=carry_sb[:, d:d + 1],
                           i=o_sb[:, d * TC + TC - 1: d * TC + TC]: e.tensor_copy(o, i))
                    res_mark["ifr%d" % rr] = [m2, m3]
                    res_mark["tD_rd%d" % rr] = [m3]
                    res_mark["scan%d" % d] = m3

            if debug and c == 0:
                dbg_gate.append(dump(dbg_o, o_sb[:], [res_mark["scan%d" % dd] for dd in range(KC)]))
                res_mark["dbg_o_gate"] = [dbg_gate[-1]]

            def g_mm(d):
                q, dt2 = d // 4, d % 4
                sl_g = panel_slot(P_WG + q)
                bgi = d % 4
                bg = "fm%d" % bgi
                bank_write(bg)
                for kc in range(KC):
                    last = kc == KC - 1
                    mg = K.emit("tensor", lambda e, o=fm[bgi][:],
                                l=sl_g[:, kc * 512 + dt2 * 128: kc * 512 + (dt2 + 1) * 128],
                                r=xn_sb[:, xnb + kc * TC: xnb + (kc + 1) * TC], s=(kc == 0), p=last:
                                e.matmul(o, l, r, start=s, stop=p), SP if last else None, 1)
                bank_st[bg]["write"] = mg
                if dt2 == 3:
                    panel_done(P_WG + q, mg)

            def g_gate(d):
                bgi = d % 4
                bg = "fm%d" % bgi
                rr = d % 2
                K.waits("scalar", res_mark.get("sgr%d" % rr, []))
                bank_read("scalar", bg)
                msg = K.emit("scalar", lambda e, o=sg_sb[rr][:], i=fm[bgi][:]:
                             e.activation(o, i, AF.Silu))
                bank_read_done(bg, msg)
                K.wait("vector", msg)
                K.wait("vector", m_gn)
                bank_read("vector", "pt0")
                K.waits("vector", res_mark.get("tE_rd%d" % rr, []))
                mt = K.emit("vector", lambda e, o=tE32[:], i=o_sb[:, d * TC:(d + 1) * TC],
                            s_=gn_sb[:, d:d + 1], b=pt[0][:, 0:512]:
                            e.scalar_tensor_tensor(o, i, s_, b, ALU.mult, ALU.mult))
                bank_read_done("pt0", mt)
                K.waits("vector", dbg_gate)
                mog = K.emit("vector", lambda e, o=o_sb[:, d * TC:(d + 1) * TC], a=tE32[:],
                             b=sg_sb[rr][:]: e.tensor_mul(o, a, b))
                res_mark["sgr%d" % rr] = [mog]
                res_mark["tE_rd%d" % rr] = [mog]
                res_mark["og%d" % d] = mog

            sq_sq(KC - 3)
            g_mm(0)
            sq_mm(KC - 3)
            sq_sq(KC - 2)
            g_mm(1)
            sq_mm(KC - 2)
            sq_sq(KC - 1)
            g_mm(2)
            mon15 = sq_mm(KC - 1)
            g_mm(3)
            bank_st["pt0"]["write"] = mon15

            bank_read("vector", "pt0")
            K.emit("vector", lambda e: e.tensor_scalar(rsq_sb[:], pt[0][0:1, 0:512], 1.0 / H, EPS,
                                                       ALU.mult, ALU.add))
            m_rc = K.emit("vector", lambda e: e.tensor_copy(rsq_sb[:], rsq_sb[:]))
            K.wait("scalar", m_rc)
            m_rs = K.emit("scalar", lambda e: e.activation(rsq_sb[:], rsq_sb[:], AF.Sqrt))
            K.wait("vector", m_rs)
            m_rq = K.emit("vector", lambda e: e.reciprocal(rsq_sb[:], rsq_sb[:]))
            bank_read_done("pt0", m_rq)
            bank_write("pt0")
            K.wait("tensor", m_rq)
            m_rb = K.emit("tensor", lambda e: e.matmul(pt[0][:, 0:512], oner_sb[:],
                                                       rsq_sb[:], start=True, stop=True), SP, 1)
            bank_st["pt0"]["write"] = m_rb
            for d in range(4):
                g_gate(d)
            for d in range(4, KC):
                g_mm(d)
                g_gate(d)

            og_all = res_mark["og15"]
            if debug and c == 0:
                dbg_gate.append(dump(dbg_og, o_sb[:], [og_all]))
                dbg_gate.append(dump(dbg_rsq, rsq_sb[:], [m_rq]))

            if c >= 1:
                K.wait("scalar", res_mark["out_store"])
                K.wait("scalar", (SXA, SXA.n))
                m_oax = K.emit("scalar", lambda e, d=oa_sb[:].rearrange("p (n f) -> p n f", n=NT),
                               s=x_d[c * TC:(c + 1) * TC, :].rearrange("(n p) f -> p n f", p=128):
                               e.dma_start(d, s), SXA, 16)
                res_mark["oa_x"] = m_oax

            for q in range(4):
                p_g = P_SHG + 2 * q
                p_u = P_SHG + 2 * q + 1
                sl_g = panel_slot(p_g)
                sl_u = panel_slot(p_u)
                for ft2 in range(4):
                    fs = q * 4 + ft2
                    bgi = 2 * (ft2 % 2)
                    bui = bgi + 1
                    bg = "fm%d" % bgi
                    bu = "fm%d" % bui
                    bank_write(bg)
                    for kc in range(KC):
                        last = kc == KC - 1
                        mg = K.emit("tensor", lambda e, o=fm[bgi][:],
                                    l=sl_g[:, kc * 512 + ft2 * 128: kc * 512 + (ft2 + 1) * 128],
                                    r=xn_sb[:, xnb + kc * TC: xnb + (kc + 1) * TC], s=(kc == 0), p=last:
                                    e.matmul(o, l, r, start=s, stop=p), SP if last else None, 1)
                    bank_st[bg]["write"] = mg
                    bank_write(bu)
                    for kc in range(KC):
                        last = kc == KC - 1
                        mu = K.emit("tensor", lambda e, o=fm[bui][:],
                                    l=sl_u[:, kc * 512 + ft2 * 128: kc * 512 + (ft2 + 1) * 128],
                                    r=xn_sb[:, xnb + kc * TC: xnb + (kc + 1) * TC], s=(kc == 0), p=last:
                                    e.matmul(o, l, r, start=s, stop=p), SP if last else None, 1)
                    bank_st[bu]["write"] = mu
                    if ft2 == 3:
                        panel_done(p_g, mg)
                        panel_done(p_u, mu)
                    rr = fs % 2
                    K.waits("scalar", res_mark.get("tD_rd%d" % rr, []))
                    bank_read("scalar", bg)
                    m1 = K.emit("scalar", lambda e, o=tD[rr][:], i=fm[bgi][:]:
                                e.activation(o, i, AF.Silu))
                    bank_read_done(bg, m1)
                    K.wait("vector", m1)
                    bank_read("vector", bu)
                    K.waits("vector", res_mark.get("act_wr", []))
                    K.waits("vector", dbg_gate)
                    m2 = K.emit("vector", lambda e, o=act_sb[:, fs * TC:(fs + 1) * TC], a=tD[rr][:],
                                b=fm[bui][:]: e.tensor_mul(o, a, b))
                    bank_read_done(bu, m2)
                    res_mark["tD_rd%d" % rr] = [m2]
                    res_mark["acts%d" % fs] = m2
            res_mark["act_wr"] = []

            K.wait("vector", res_mark["oa_x"])
            for q in range(4):
                p_wo = P_WO + q
                sl_o = panel_slot(p_wo)
                for tt in range(NT):
                    pbi = ptn[0] % 2
                    pb = "pt%d" % pbi
                    ptn[0] += 1
                    bank_write(pb)
                    K.wait("tensor", og_all)
                    for kc in range(KC):
                        last = kc == KC - 1
                        mwo = K.emit("tensor", lambda e, o=pt[pbi][:, 0:512],
                                     l=o_sb[:, kc * TC + tt * 128: kc * TC + (tt + 1) * 128],
                                     r=sl_o[:, kc * 512:(kc + 1) * 512], s=(kc == 0), p=last:
                                     e.matmul(o, l, r, start=s, stop=p), SP if last else None, 1)
                    bank_st[pb]["write"] = mwo
                    if tt == NT - 1:
                        panel_done(p_wo, mwo)
                    bank_read("vector", pb)
                    oa_sl = oa_sb[:, tt * H + q * 512: tt * H + (q + 1) * 512]
                    mev = K.emit("vector", lambda e, o=oa_sl, a=oa_sl, b=pt[pbi][:, 0:512]:
                                 e.tensor_add(o, a, b))
                    bank_read_done(pb, mev)
                    oa_reads.append(mev)
            res_mark["wo_done"] = (SP, SP.n)
            if debug and c == 0:
                dbg_gate.append(dump(dbg_oa1, oa_sb[:], list(oa_reads)))
                dbg_gate.append(dump(dbg_act, act_sb[:], [res_mark["acts15"]]))

            actsh_all = res_mark["acts15"]
            for q in range(4):
                p_d = P_SHD + q
                sl_d = panel_slot(p_d)
                for tt in range(NT):
                    pbi = ptn[0] % 2
                    pb = "pt%d" % pbi
                    ptn[0] += 1
                    bank_write(pb)
                    K.wait("tensor", actsh_all)
                    for kc in range(KC):
                        last = kc == KC - 1
                        md = K.emit("tensor", lambda e, o=pt[pbi][:, 0:512],
                                    l=act_sb[:, kc * TC + tt * 128: kc * TC + (tt + 1) * 128],
                                    r=sl_d[:, kc * 512:(kc + 1) * 512], s=(kc == 0), p=last:
                                    e.matmul(o, l, r, start=s, stop=p), SP if last else None, 1)
                    bank_st[pb]["write"] = md
                    if tt == NT - 1:
                        panel_done(p_d, md)
                    bank_read("vector", pb)
                    oa_sl = oa_sb[:, tt * H + q * 512: tt * H + (q + 1) * 512]
                    K.waits("vector", dbg_gate)
                    mev = K.emit("vector", lambda e, o=oa_sl, a=oa_sl, b=pt[pbi][:, 0:512]:
                                 e.tensor_add(o, a, b))
                    bank_read_done(pb, mev)
                    oa_reads.append(mev)
            res_mark["exp_act_rd0"] = [(SP, SP.n)]
            res_mark["exp_act_rd1"] = [(SP, SP.n)]
            if debug and c == 0:
                dbg_gate.append(dump(dbg_oa2, oa_sb[:], list(oa_reads)))

            def emit_gu(n):
                half = n % 2
                abase = half * 8 * TC
                K.waits("vector", res_mark.get("exp_act_rd%d" % half, []))
                amarks = []
                for h2 in range(2):
                    p_g = P_GU[n] + 2 * h2
                    p_u = P_GU[n] + 2 * h2 + 1
                    sl_g = panel_slot(p_g)
                    sl_u = panel_slot(p_u)
                    for ft2 in range(4):
                        fc = h2 * 4 + ft2
                        bgi = 2 * (ft2 % 2)
                        bui = bgi + 1
                        bg = "fm%d" % bgi
                        bu = "fm%d" % bui
                        bank_write(bg)
                        for kc in range(KC):
                            last = kc == KC - 1
                            mg = K.emit("tensor", lambda e, o=fm[bgi][:],
                                        l=sl_g[:, kc * 512 + ft2 * 128: kc * 512 + (ft2 + 1) * 128],
                                        r=xn_sb[:, xnb + kc * TC: xnb + (kc + 1) * TC],
                                        s=(kc == 0), p=last:
                                        e.matmul(o, l, r, start=s, stop=p), SP if last else None, 1)
                        bank_st[bg]["write"] = mg
                        bank_write(bu)
                        for kc in range(KC):
                            last = kc == KC - 1
                            mu = K.emit("tensor", lambda e, o=fm[bui][:],
                                        l=sl_u[:, kc * 512 + ft2 * 128: kc * 512 + (ft2 + 1) * 128],
                                        r=xn_sb[:, xnb + kc * TC: xnb + (kc + 1) * TC],
                                        s=(kc == 0), p=last:
                                        e.matmul(o, l, r, start=s, stop=p), SP if last else None, 1)
                        bank_st[bu]["write"] = mu
                        if ft2 == 3:
                            panel_done(p_g, mg)
                            panel_done(p_u, mu)
                        rr = fc % 2
                        K.waits("scalar", res_mark.get("tD_rd%d" % rr, []))
                        bank_read("scalar", bg)
                        m1 = K.emit("scalar", lambda e, o=tD[rr][:], i=fm[bgi][:]:
                                    e.activation(o, i, AF.Silu))
                        bank_read_done(bg, m1)
                        K.wait("vector", m1)
                        bank_read("vector", bu)
                        m2 = K.emit("vector", lambda e,
                                    o=act_sb[:, abase + fc * TC: abase + (fc + 1) * TC],
                                    a=tD[rr][:], b=fm[bui][:]: e.tensor_mul(o, a, b))
                        bank_read_done(bu, m2)
                        res_mark["tD_rd%d" % rr] = [m2]
                        amarks.append(m2)
                res_mark["exp_act%d" % n] = amarks[-1]

            def emit_down(n):
                half = n % 2
                abase = half * 8 * TC
                act_all = res_mark["exp_act%d" % n]
                dreads = []
                for dp in range(2):
                    p_d = P_D[n] + dp
                    sl_d = panel_slot(p_d)
                    for tt in range(NT):
                        pbi = ptn[0] % 2
                        pb = "pt%d" % pbi
                        ptn[0] += 1
                        bank_write(pb)
                        K.wait("tensor", act_all)
                        for fc in range(8):
                            for jh in range(2):
                                last = fc == 7 and jh == 1
                                md = K.emit("tensor", lambda e,
                                            o=pt[pbi][:, jh * 512:(jh + 1) * 512],
                                            l=act_sb[:, abase + fc * TC + tt * 128: abase + fc * TC + (tt + 1) * 128],
                                            r=sl_d[:, fc * 1024 + jh * 512: fc * 1024 + (jh + 1) * 512],
                                            s=(fc == 0), p=(fc == 7):
                                            e.matmul(o, l, r, start=s, stop=p), SP if last else None, 1)
                        bank_st[pb]["write"] = md
                        dreads.append(md)
                        if tt == NT - 1:
                            panel_done(p_d, md)
                        bank_read("vector", pb)
                        oa_sl = oa_sb[:, tt * H + dp * 1024: tt * H + (dp + 1) * 1024]
                        w_ap = wexp_sb[:, par * 64 + tt * NE + n: par * 64 + tt * NE + n + 1]
                        K.waits("vector", dbg_gate)
                        mev = K.emit("vector", lambda e, o=oa_sl, a=pt[pbi][:], w=w_ap, b=oa_sl:
                                     e.scalar_tensor_tensor(o, a, w, b, ALU.mult, ALU.add))
                        bank_read_done(pb, mev)
                        oa_reads.append(mev)
                res_mark["exp_act_rd%d" % half] = [dreads[-1]]

            for n in range(NE):
                if 5 <= n <= 8 and c + 1 < NCH:
                    head_dma(c + 1, n - 5)
                if 6 <= n <= 9 and c + 1 < NCH:
                    head_ln(c + 1, n - 6)
                emit_gu(n)
                if 6 <= n <= 9 and c + 1 < NCH:
                    head_tr(c + 1, n - 6)
                if n >= 1:
                    emit_down(n - 1)
            emit_down(NE - 1)
            res_mark["act_wr"] = [(SP, SP.n)]
            res_mark["xn_rd%d" % par] = [(SP, SP.n)]

            K.waits("sync", oa_reads)
            K.wait("sync", (SOUT, SOUT.n))
            m_out = K.emit("sync", lambda e,
                           d=out_d[row0:row0 + TC, :].rearrange("(n p) f -> p n f", p=128),
                           s=oa_sb[:].rearrange("p (n f) -> p n f", n=NT): e.dma_start(d, s),
                           SOUT, 16)
            res_mark["out_store"] = m_out

        K.wait("sync", (SOUT, SOUT.n))
        K.wait("sync", (SX, SX.n))
        K.wait("sync", (SXA, SXA.n))
        K.wait("sync", (SPR, SPR.n))
        K.wait("sync", (SWR, SWR.n))
        K.wait("sync", (SXT, SXT.n))
        for i in range(4):
            K.wait("sync", (SLg[i], SLg[i].n))
            K.wait("sync", (SLs[i], SLs[i].n))
            K.wait("sync", (SW[i], SW[i].n))
        if debug:
            K.wait("sync", (SDB, SDB.n))

        for sem_, cnt_ in K.all_waits:
            assert cnt_ <= sem_.n, f"wait target {cnt_} exceeds final count {sem_.n} -> would hang"

        with nc.Block() as block:
            K.flush(block)

    return nc


_NC_CACHE = [None]


def kernel(**inp):
    x = np.ascontiguousarray(np.asarray(inp["L0_attn_hidden_states"], dtype=np.float32))
    t = lambda a: np.ascontiguousarray(np.asarray(a, dtype=np.float32).T)
    wi_t = t(inp["L0_attn_Wi"])
    wf_t = t(inp["L0_attn_Wf"])
    wg_t = t(inp["L0_attn_Wg"])
    wo_t = t(inp["L0_attn_Wo"])
    wr_t = t(inp["L0_ffn_gate_weight"])
    gnorm = np.asarray(inp["L0_attn_gnorm_w"], dtype=np.float32)
    gnorm_fm = np.ascontiguousarray(gnorm.reshape(KC, 128).T)
    bias_r = np.ascontiguousarray(np.asarray(inp["L0_ffn_e_score_correction_bias"], dtype=np.float32))
    tr3 = lambda a: np.ascontiguousarray(np.asarray(a, dtype=np.float32).transpose(0, 2, 1))
    gw_t = tr3(inp["L0_ffn_gate_w"])
    uw_t = tr3(inp["L0_ffn_up_w"])
    dw_t = tr3(inp["L0_ffn_down_w"])
    shg_t = t(inp["L0_ffn_sh_gate_w"])
    shu_t = t(inp["L0_ffn_sh_up_w"])
    shd_t = t(inp["L0_ffn_sh_down_w"])

    if _NC_CACHE[0] is None:
        _NC_CACHE[0] = build_core()
    nc = _NC_CACHE[0]

    common = dict(wi_t=wi_t, wf_t=wf_t, wg_t=wg_t, wo_t=wo_t, wr_t=wr_t,
                  gnorm_fm=gnorm_fm, bias_r=bias_r, gw_t=gw_t, uw_t=uw_t,
                  dw_t=dw_t, shg_t=shg_t, shu_t=shu_t, shd_t=shd_t)
    in_maps = [dict(x=np.ascontiguousarray(x[b]), **common) for b in range(8)]
    res = run_bass_kernel_spmd(nc, in_maps, core_ids=list(range(8)))
    return np.stack([res.results[b]["out"] for b in range(8)], axis=0)


# version 13
# speedup vs baseline: 1.1207x; 1.1207x over previous
import contextlib
import numpy as np
import concourse.bass as bass
import concourse.mybir as mybir
from concourse.bass_utils import run_bass_kernel_spmd

F32 = mybir.dt.float32
BF16 = mybir.dt.bfloat16
AF = mybir.ActivationFunctionType
ALU = mybir.AluOpType
AX = mybir.AxisListType

H = 2048
KC = H // 128
TC = 512
NCH = 8
NT = TC // 128
NE = 16
FE = 1024
BIG = 1e30
EPS = 1e-5

ENGS = ("tensor", "vector", "scalar", "sync", "gpsimd")


class Sem:
    def __init__(self, h):
        self.h = h
        self.n = 0


class B:

    def __init__(self, nc):
        self.nc = nc
        self.prog = {e: [] for e in ENGS}
        self.wc = {e: {} for e in ENGS}
        self.all_waits = []
        self.auto = {}

    def wait(self, eng, mark):
        if mark is None:
            return
        sem, cnt = mark
        key = id(sem)
        if self.wc[eng].get(key, 0) >= cnt:
            return
        self.wc[eng][key] = cnt
        self.all_waits.append((sem, cnt))

        def th(e, s=sem, c=cnt):
            e.wait_ge(s.h, c)

        self.prog[eng].append(th)

    def waits(self, eng, marks):
        best = {}
        for m in marks:
            if m is None:
                continue
            sem, cnt = m
            k = id(sem)
            if k not in best or best[k][1] < cnt:
                best[k] = m
        for m in best.values():
            self.wait(eng, m)

    def emit(self, eng, fn, sem=None, n=1):
        a = self.auto.get(eng)
        if a is not None:
            self.wait(eng, (a, a.n))
            if sem is None:
                sem, n = a, 1
        mark = None
        if sem is not None:
            sem.n += n
            mark = (sem, sem.n)

        def th(e, fn=fn, sem=sem, n=n):
            ins = fn(e)
            if sem is not None:
                ins.then_inc(sem.h, n)

        self.prog[eng].append(th)
        return mark

    def flush(self, block):
        for eng in ENGS:
            prog = self.prog[eng]
            deco = getattr(block, eng)

            def body(e, prog=prog):
                for th in prog:
                    th(e)

            deco(body)


def build_core(debug=False):
    nc = bass.Bass("TRN2", target_bir_lowering=False)

    x_d = nc.dram_tensor("x", [NCH * TC, H], F32, kind="ExternalInput").ap()
    wi_d = nc.dram_tensor("wi_t", [H, H], F32, kind="ExternalInput").ap()
    wf_d = nc.dram_tensor("wf_t", [H, H], F32, kind="ExternalInput").ap()
    wg_d = nc.dram_tensor("wg_t", [H, H], F32, kind="ExternalInput").ap()
    wo_d = nc.dram_tensor("wo_t", [H, H], F32, kind="ExternalInput").ap()
    wr_d = nc.dram_tensor("wr_t", [H, NE], F32, kind="ExternalInput").ap()
    gn_d = nc.dram_tensor("gnorm_fm", [128, KC], F32, kind="ExternalInput").ap()
    bias_d = nc.dram_tensor("bias_r", [NE], F32, kind="ExternalInput").ap()
    gw_d = nc.dram_tensor("gw_t", [NE, H, FE], F32, kind="ExternalInput").ap()
    uw_d = nc.dram_tensor("uw_t", [NE, H, FE], F32, kind="ExternalInput").ap()
    dw_d = nc.dram_tensor("dw_t", [NE, FE, H], F32, kind="ExternalInput").ap()
    shg_d = nc.dram_tensor("shg_t", [H, H], F32, kind="ExternalInput").ap()
    shu_d = nc.dram_tensor("shu_t", [H, H], F32, kind="ExternalInput").ap()
    shd_d = nc.dram_tensor("shd_t", [H, H], F32, kind="ExternalInput").ap()
    out_d = nc.dram_tensor("out", [NCH * TC, H], F32, kind="ExternalOutput").ap()
    if debug:
        dbg_xn = nc.dram_tensor("dbg_xn", [128, KC * TC], F32, kind="ExternalOutput").ap()
        dbg_sc = nc.dram_tensor("dbg_sc", [128, NT * NE], F32, kind="ExternalOutput").ap()
        dbg_wx = nc.dram_tensor("dbg_wx", [128, NT * NE], F32, kind="ExternalOutput").ap()
        dbg_o = nc.dram_tensor("dbg_o", [128, KC * TC], F32, kind="ExternalOutput").ap()
        dbg_og = nc.dram_tensor("dbg_og", [128, KC * TC], F32, kind="ExternalOutput").ap()
        dbg_rsq = nc.dram_tensor("dbg_rsq", [1, 512], F32, kind="ExternalOutput").ap()
        dbg_oa1 = nc.dram_tensor("dbg_oa1", [128, NT * H], F32, kind="ExternalOutput").ap()
        dbg_oa2 = nc.dram_tensor("dbg_oa2", [128, NT * H], F32, kind="ExternalOutput").ap()
        dbg_act = nc.dram_tensor("dbg_act", [128, KC * TC], F32, kind="ExternalOutput").ap()

    def pan2(ap, c0, cols, kc):
        return ap[:, c0:c0 + cols].rearrange("(c p) f -> p c f", p=128)

    panels = []
    P_WIF = len(panels)
    for q in range(4):
        panels.append(pan2(wi_d, q * 512, 512, KC))
        panels.append(pan2(wf_d, q * 512, 512, KC))
    P_WG = len(panels)
    for q in range(4):
        panels.append(pan2(wg_d, q * 512, 512, KC))
    P_SHG = len(panels)
    for q in range(4):
        panels.append(pan2(shg_d, q * 512, 512, KC))
        panels.append(pan2(shu_d, q * 512, 512, KC))
    P_WO = len(panels)
    for q in range(4):
        panels.append(pan2(wo_d, q * 512, 512, KC))
    P_SHD = len(panels)
    for q in range(4):
        panels.append(pan2(shd_d, q * 512, 512, KC))
    P_GU = {}
    P_D = {}
    for n in range(NE):
        P_GU[n] = len(panels)
        for h2 in range(2):
            panels.append(pan2(gw_d[n], h2 * 512, 512, KC))
            panels.append(pan2(uw_d[n], h2 * 512, 512, KC))
        if n >= 1:
            P_D[n - 1] = len(panels)
            for dp in range(2):
                panels.append(pan2(dw_d[n - 1], dp * 1024, 1024, 8))
    P_D[NE - 1] = len(panels)
    for dp in range(2):
        panels.append(pan2(dw_d[NE - 1], dp * 1024, 1024, 8))
    NPAN = len(panels)
    KCN = [KC] * NPAN
    for n in range(NE):
        KCN[P_D[n]] = 8
        KCN[P_D[n] + 1] = 8

    wc_d = nc.dram_tensor("wcache", [NPAN, 128, 8192], BF16, kind="Internal").ap()

    with contextlib.ExitStack() as st:
        E = st.enter_context

        xn_sb = E(nc.sbuf_tensor("xn_sb", [128, 2 * KC * TC], BF16))
        o_sb = E(nc.sbuf_tensor("o_sb", [128, KC * TC], BF16))
        oa_sb = E(nc.sbuf_tensor("oa_sb", [128, NT * H], F32))
        act_sb = E(nc.sbuf_tensor("act_sb", [128, KC * TC], BF16))
        slots = [E(nc.sbuf_tensor("slot%d" % i, [128, 8192], BF16)) for i in range(4)]
        xln = E(nc.sbuf_tensor("xln", [128, H], F32))
        xbf = E(nc.sbuf_tensor("xbf", [128, H], BF16))
        xtok = [E(nc.sbuf_tensor("xtok%d" % i, [128, H], F32)) for i in range(2)]
        tA = [E(nc.sbuf_tensor("tA%d" % i, [128, 512], BF16)) for i in range(2)]
        tB = [E(nc.sbuf_tensor("tB%d" % i, [128, 512], BF16)) for i in range(2)]
        tC = [E(nc.sbuf_tensor("tC%d" % i, [128, 512], BF16)) for i in range(2)]
        tD = [E(nc.sbuf_tensor("tD%d" % i, [128, 512], BF16)) for i in range(2)]
        tE32 = E(nc.sbuf_tensor("tE32", [128, 512], F32))
        sg_sb = [E(nc.sbuf_tensor("sg%d" % i, [128, 512], BF16)) for i in range(2)]
        sqt_sb = [E(nc.sbuf_tensor("sqt%d" % i, [128, 512], BF16)) for i in range(2)]
        carry_sb = E(nc.sbuf_tensor("carry_sb", [128, KC], F32))
        scores_sb = E(nc.sbuf_tensor("scores_sb", [128, 2 * NT * NE], F32))
        wexp_sb = E(nc.sbuf_tensor("wexp_sb", [128, 2 * NT * NE], F32))
        rt_sb = E(nc.sbuf_tensor("rt_sb", [128, 176], F32))
        st_sb = E(nc.sbuf_tensor("st_sb", [128, 16], F32))
        gn_sb = E(nc.sbuf_tensor("gn_sb", [128, KC], F32))
        bb_sb = E(nc.sbuf_tensor("bb_sb", [128, NE], F32))
        wr_sb = E(nc.sbuf_tensor("wr_sb", [128, KC * NE], BF16))
        idn_sb = E(nc.sbuf_tensor("idn_sb", [128, 128], F32))
        onec_sb = E(nc.sbuf_tensor("onec_sb", [128, 1], BF16))
        oner_sb = E(nc.sbuf_tensor("oner_sb", [1, 128], F32))
        rsq_sb = E(nc.sbuf_tensor("rsq_sb", [1, 512], F32))
        eps_sb = E(nc.sbuf_tensor("eps_sb", [128, 1], F32))
        brow_sb = E(nc.sbuf_tensor("brow_sb", [1, NE], F32))

        fm = [E(nc.psum_tensor("fm%d" % i, [128, 512], F32)) for i in range(4)]
        pt = [E(nc.psum_tensor("pt%d" % i, [128, 1024], F32)) for i in range(2)]

        SP = Sem(E(nc.semaphore("SP")))
        SV = Sem(E(nc.semaphore("SV")))
        SS = Sem(E(nc.semaphore("SS")))
        SG = Sem(E(nc.semaphore("SG")))
        SLg = [Sem(E(nc.semaphore("SLg%d" % i))) for i in range(4)]
        SLs = [Sem(E(nc.semaphore("SLs%d" % i))) for i in range(4)]
        SW = [Sem(E(nc.semaphore("SW%d" % i))) for i in range(4)]
        SWR = Sem(E(nc.semaphore("SWR")))
        SX = Sem(E(nc.semaphore("SX")))
        SXA = Sem(E(nc.semaphore("SXA")))
        SXT = Sem(E(nc.semaphore("SXT")))
        SOUT = Sem(E(nc.semaphore("SOUT")))
        SPR = Sem(E(nc.semaphore("SPR")))
        SDB = Sem(E(nc.semaphore("SDB")))

        K = B(nc)
        K.auto = {"vector": SV, "scalar": SS}

        slot_st = [{"load": None, "readers": []} for _ in range(4)]
        bank_st = {nm: {"write": None, "readers": []}
                   for nm in ("fm0", "fm1", "fm2", "fm3", "pt0", "pt1")}
        res_mark = {}

        def bank_write(nm, extra=()):
            s = bank_st[nm]
            K.waits("tensor", s["readers"] + list(extra))
            s["readers"] = []

        def bank_read(eng, nm):
            K.wait(eng, bank_st[nm]["write"])

        def bank_read_done(nm, mark):
            bank_st[nm]["readers"].append(mark)

        cursor = [0]

        def load_next_panel():
            g = cursor[0]
            if g >= NCH * NPAN:
                return
            cursor[0] += 1
            c, p = divmod(g, NPAN)
            sl = p % 4
            stt = slot_st[sl]
            fp32 = c == 0 or (c == 1 and p % 2 == 1)
            eng = "gpsimd" if (fp32 or p % 2 == 1) else "sync"
            sem = SLg[sl] if eng == "gpsimd" else SLs[sl]
            K.waits(eng, stt["readers"])
            K.wait(eng, (SLg[sl], SLg[sl].n))
            K.wait(eng, (SLs[sl], SLs[sl].n))
            if fp32:
                dst = slots[sl][:].rearrange("p (c f) -> p c f", c=KCN[p])
                lm = K.emit(eng, lambda e, d=dst, s=panels[p]: e.dma_start(d, s), sem, 16)
            else:
                lm = K.emit(eng, lambda e, d=slots[sl][:], s=wc_d[p]: e.dma_start(d, s), sem, 16)
            stt["load"] = lm
            stt["readers"] = []
            if fp32 and (c == 0) == (p % 2 == 0):
                K.wait("sync", lm)
                K.wait("sync", (SW[sl], SW[sl].n))
                wm = K.emit("sync", lambda e, d=wc_d[p], s=slots[sl][:]: e.dma_start(d, s), SW[sl], 16)
                stt["readers"].append(wm)

        def panel_slot(p):
            sl = p % 4
            K.wait("tensor", slot_st[sl]["load"])
            return slots[sl]

        def panel_done(p, mark):
            slot_st[p % 4]["readers"].append(mark)
            load_next_panel()

        tr_bank = [0]
        prologue_marks = {}

        def head_dma(c, tt):
            row0 = c * TC
            deng = "sync" if c == 0 else "scalar"
            xt_in = xtok[tt % 2]
            K.waits(deng, res_mark.get("xin_rd%d" % (tt % 2), []))
            K.wait(deng, (SX, SX.n))
            m_x = K.emit(deng, lambda e, d=xt_in[:],
                         s=x_d[row0 + tt * 128: row0 + (tt + 1) * 128, :]:
                         e.dma_start(d, s), SX, 16)
            res_mark["head_x%d" % (tt % 2)] = m_x

        def head_ln(c, tt):
            m_eps = prologue_marks["eps"]
            m_x = res_mark["head_x%d" % (tt % 2)]
            xt_in = xtok[tt % 2]
            sb = 8 * (tt % 2)
            rs = st_sb[:, sb + 0:sb + 1]; qs = st_sb[:, sb + 1:sb + 2]
            mmn = st_sb[:, sb + 2:sb + 3]; msq = st_sb[:, sb + 3:sb + 4]
            var = st_sb[:, sb + 4:sb + 5]; stdv = st_sb[:, sb + 5:sb + 6]
            rstd = st_sb[:, sb + 6:sb + 7]; negb = st_sb[:, sb + 7:sb + 8]
            K.wait("vector", m_x)
            K.waits("vector", res_mark.get("st_rd%d" % (tt % 2), []))
            m_rs = K.emit("vector", lambda e, o=rs, i=xt_in[:]: e.reduce_sum(o, i, AX.X))
            K.wait("scalar", m_x)
            K.waits("scalar", res_mark.get("qs_rd%d" % (tt % 2), []))
            m_sq = K.emit("scalar", lambda e, o=xln[:], i=xt_in[:], a=qs:
                          e.activation(o, i, AF.Square, accum_out=a))
            K.wait("vector", m_sq)
            K.emit("vector", lambda e, o=mmn, i=rs: e.tensor_scalar_mul(o, i, 1.0 / H))
            K.emit("vector", lambda e, o=msq, a=mmn: e.tensor_mul(o, a, a))
            K.emit("vector", lambda e, o=var, i=qs: e.tensor_scalar_mul(o, i, 1.0 / H))
            m_v = K.emit("vector", lambda e, o=var, a=var, b=msq: e.tensor_sub(o, a, b))
            res_mark["qs_rd%d" % (tt % 2)] = [m_v]
            K.wait("scalar", m_v)
            K.wait("scalar", m_eps)
            m_std = K.emit("scalar", lambda e, o=stdv, i=var:
                           e.activation(o, i, AF.Sqrt, bias=eps_sb[:]))
            K.wait("vector", m_std)
            K.emit("vector", lambda e, o=rstd, i=stdv: e.reciprocal(o, i))
            K.emit("vector", lambda e, o=negb, a=mmn, b=rstd: e.tensor_mul(o, a, b))
            m_nb = K.emit("vector", lambda e, o=negb, i=negb: e.tensor_scalar_mul(o, i, -1.0))
            K.wait("scalar", m_nb)
            K.waits("scalar", res_mark.get("xbf_rd", []))
            m_app = K.emit("scalar", lambda e, o=xbf[:], i=xt_in[:], sc=rstd, b=negb:
                           e.activation(o, i, AF.Identity, bias=b, scale=sc))
            res_mark["st_rd%d" % (tt % 2)] = [m_app]
            res_mark["xin_rd%d" % (tt % 2)] = [m_rs, m_app]
            K.waits("scalar", res_mark.get("xn_rd%d" % (c % 2), []))
            K.wait("scalar", (SXT, SXT.n))
            xn5 = xn_sb[:].rearrange("p (pr hc tl t) -> p pr hc tl t", pr=2, hc=KC, t=128)
            m_xt = K.emit("scalar", lambda e, d=xn5[:, c % 2, :, tt, :], s=xbf[:]:
                          e.dma_start(d, s, transpose=True), SXT, 16)
            res_mark["xbf_rd"] = [m_xt]
            res_mark["head_xt"] = m_xt
            if tt == NT - 1:
                res_mark["xn_all%d" % (c % 2)] = m_xt

        def head_tr(c, tt):
            par = c % 2
            xnb = par * KC * TC
            m_wr = prologue_marks["wr"]
            m_bbc = prologue_marks["bbc"]
            m_xt = res_mark["head_xt"]
            bk = tr_bank[0] % 4
            tr_bank[0] += 1
            nm = "fm%d" % bk
            bank_write(nm)
            K.wait("tensor", m_wr)
            K.wait("tensor", m_xt)
            for kc in range(KC):
                is_l = kc == KC - 1
                mr = K.emit("tensor", lambda e, o=fm[bk][:, 0:NE],
                            l=xn_sb[:, xnb + kc * TC + tt * 128: xnb + kc * TC + (tt + 1) * 128],
                            r=wr_sb[:, kc * NE:(kc + 1) * NE], s=(kc == 0), p=is_l:
                            e.matmul(o, l, r, start=s, stop=p), SP if is_l else None, 1)
            bank_st[nm]["write"] = mr
            bank_read("scalar", nm)
            m_sc = K.emit("scalar", lambda e,
                          o=scores_sb[:, par * 64 + tt * NE: par * 64 + (tt + 1) * NE],
                          i=fm[bk][:, 0:NE]: e.activation(o, i, AF.Sigmoid))
            bank_read_done(nm, m_sc)
            K.wait("vector", m_sc)
            K.wait("vector", m_bbc)
            S_ = scores_sb[:, par * 64 + tt * NE: par * 64 + (tt + 1) * NE]
            r1 = rt_sb[:, 0:16]; r3 = rt_sb[:, 16:32]; r4 = rt_sb[:, 32:48]
            r8 = rt_sb[:, 48:64]; r9 = rt_sb[:, 64:80]; r10 = rt_sb[:, 80:96]
            r11 = rt_sb[:, 96:112]; r12 = rt_sb[:, 112:128]; r13 = rt_sb[:, 128:144]
            g4 = rt_sb[:, 144:148]; g4b = rt_sb[:, 148:152]; gsum = rt_sb[:, 152:156]
            gm = rt_sb[:, 156:160]; e1 = rt_sb[:, 160:164]; e2 = rt_sb[:, 164:168]
            sc5 = rt_sb[:, 168:176]
            V = "vector"
            K.emit(V, lambda e, S=S_: e.tensor_add(r1, S, bb_sb[:]))
            r1v = r1.rearrange("p (g e) -> p g e", e=4)
            r3v = r3.rearrange("p (g e) -> p g e", e=4)
            r4v = r4.rearrange("p (g e) -> p g e", e=4)
            K.emit(V, lambda e: e.reduce_max(g4, r1v, AX.X))
            K.emit(V, lambda e: e.tensor_tensor(r3v, r1v, g4.unsqueeze(2).broadcast_to([128, 4, 4]),
                                                ALU.is_equal))
            K.emit(V, lambda e: e.scalar_tensor_tensor(r4v, r3v, -BIG, r1v, ALU.mult, ALU.add))
            K.emit(V, lambda e: e.reduce_max(g4b, r4v, AX.X))
            K.emit(V, lambda e: e.tensor_add(gsum, g4, g4b))
            K.emit(V, lambda e: e.reduce_max(gm[:, 0:1], gsum, AX.X))
            K.emit(V, lambda e: e.tensor_scalar(e1, gsum, gm[:, 0:1], None, ALU.is_equal))
            K.emit(V, lambda e: e.scalar_tensor_tensor(gsum, e1, -BIG, gsum, ALU.mult, ALU.add))
            K.emit(V, lambda e: e.reduce_max(gm[:, 1:2], gsum, AX.X))
            K.emit(V, lambda e: e.tensor_scalar(e2, gsum, gm[:, 1:2], None, ALU.is_equal))
            K.emit(V, lambda e: e.tensor_add(e1, e1, e2))
            r8v = r8.rearrange("p (g e) -> p g e", e=4)
            K.emit(V, lambda e: e.tensor_copy(r8v, e1.unsqueeze(2).broadcast_to([128, 4, 4])))
            K.emit(V, lambda e: e.tensor_scalar(r9, r8, BIG, -BIG, ALU.mult, ALU.add))
            K.emit(V, lambda e: e.tensor_add(r10, r1, r9))
            K.emit(V, lambda e: e.tensor_copy(r11, r10))
            for _r in range(4):
                K.emit(V, lambda e: e.reduce_max(sc5[:, 0:1], r10, AX.X))
                K.emit(V, lambda e: e.tensor_scalar(r12, r10, sc5[:, 0:1], None, ALU.is_equal))
                K.emit(V, lambda e: e.scalar_tensor_tensor(r10, r12, -BIG, r10, ALU.mult, ALU.add))
            K.emit(V, lambda e: e.tensor_tensor(r12, r10, r11, ALU.not_equal))
            K.emit(V, lambda e, S=S_: e.tensor_mul(r13, r12, S))
            K.emit(V, lambda e: e.reduce_sum(sc5[:, 1:2], r13, AX.X))
            K.emit(V, lambda e: e.tensor_scalar_add(sc5[:, 2:3], sc5[:, 1:2], 1e-20))
            K.emit(V, lambda e: e.reciprocal(sc5[:, 3:4], sc5[:, 2:3]))
            K.emit(V, lambda e, o=wexp_sb[:, par * 64 + tt * NE: par * 64 + (tt + 1) * NE]:
                   e.tensor_scalar(o, r13, sc5[:, 3:4], 2.5, ALU.mult, ALU.mult))

        def emit_head(c):
            head_dma(c, 0)
            head_dma(c, 1)
            for tt in range(NT):
                head_ln(c, tt)
                head_tr(c, tt)
                if tt + 2 < NT:
                    head_dma(c, tt + 2)

        iof_v = rt_sb[:, 0:128]
        iop_v = rt_sb[:, 128:129]
        K.emit("gpsimd", lambda e: e.iota(iof_v, [[1, 128]], channel_multiplier=0,
                                          allow_small_or_imprecise_dtypes=True), SG, 1)
        m_iof = K.emit("gpsimd", lambda e: e.iota(iop_v, [[1, 1]], channel_multiplier=1,
                                                  allow_small_or_imprecise_dtypes=True), SG, 1)
        K.wait("vector", m_iof)
        m_idn = K.emit("vector", lambda e: e.tensor_scalar(idn_sb[:], iof_v, iop_v,
                                                           None, ALU.is_equal))
        K.emit("vector", lambda e: e.memset(onec_sb[:], 1.0))
        m_oner = K.emit("vector", lambda e: e.memset(oner_sb[:], 1.0))
        m_carry0 = K.emit("vector", lambda e: e.memset(carry_sb[:], 0.0))
        m_eps = K.emit("vector", lambda e: e.memset(eps_sb[:], EPS))
        K.wait("gpsimd", (SWR, SWR.n))
        m_wr = K.emit("gpsimd", lambda e: e.dma_start(
            wr_sb[:].rearrange("p (c f) -> p c f", c=KC),
            wr_d.rearrange("(c p) f -> p c f", p=128)), SWR, 16)
        K.wait("sync", (SPR, SPR.n))
        m_gn = K.emit("sync", lambda e: e.dma_start(gn_sb[:], gn_d), SPR, 16)
        K.wait("sync", (SPR, SPR.n))
        m_br = K.emit("sync", lambda e: e.dma_start(brow_sb[:], bias_d.unsqueeze(0)), SPR, 16)
        K.wait("tensor", m_br)
        K.wait("tensor", m_oner)
        bank_write("fm0")
        m_bb = K.emit("tensor", lambda e: e.matmul(fm[0][:, 0:NE], oner_sb[:], brow_sb[:],
                                                   start=True, stop=True), SP, 1)
        bank_st["fm0"]["write"] = m_bb
        bank_read("vector", "fm0")
        m_bbc = K.emit("vector", lambda e: e.tensor_copy(bb_sb[:], fm[0][:, 0:NE]))
        bank_read_done("fm0", m_bbc)
        prologue_marks.update(idn=m_idn, wr=m_wr, bbc=m_bbc, eps=m_eps)

        emit_head(0)
        for _ in range(4):
            load_next_panel()
        K.wait("scalar", (SXA, SXA.n))
        m_oax = K.emit("scalar", lambda e, d=oa_sb[:].rearrange("p (n f) -> p n f", n=NT),
                       s=x_d[0:TC, :].rearrange("(n p) f -> p n f", p=128):
                       e.dma_start(d, s), SXA, 16)
        res_mark["oa_x"] = m_oax

        for c in range(NCH):
            par = c % 2
            xnb = par * KC * TC
            row0 = c * TC
            xn_all = res_mark["xn_all%d" % par]
            oa_reads = []
            ptn = [0]

            def dump(dst, srcbuf, wait_marks):
                K.waits("gpsimd", wait_marks)
                K.wait("gpsimd", (SDB, SDB.n))
                return K.emit("gpsimd", lambda e, d=dst, s=srcbuf: e.dma_start(d, s), SDB, 16)

            if debug and c == 0:
                dbg_gate = [dump(dbg_xn, xn_sb[:, 0:KC * TC], [xn_all]),
                            dump(dbg_sc, scores_sb[:, 0:64], [(SS, SS.n)]),
                            dump(dbg_wx, wexp_sb[:, 0:64], [(SV, SV.n)])]
            else:
                dbg_gate = []

            prev_wo = res_mark.get("wo_done")

            def sq_sq(d):
                rr = d % 2
                K.wait("scalar", res_mark["scan%d" % d])
                K.waits("scalar", res_mark.get("sqt_rd%d" % rr, []))
                msq_ = K.emit("scalar", lambda e, o=sqt_sb[rr][:], i=o_sb[:, d * TC:(d + 1) * TC]:
                              e.activation(o, i, AF.Square))
                res_mark["msq%d" % d] = msq_

            def sq_mm(d):
                rr = d % 2
                K.wait("tensor", res_mark["msq%d" % d])
                K.wait("tensor", m_carry0)
                if d == 0:
                    bank_write("pt0")
                mon = K.emit("tensor", lambda e, o=pt[0][0:1, 0:512], l=onec_sb[:], r=sqt_sb[rr][:],
                             s=(d == 0), p=(d == KC - 1): e.matmul(o, l, r, start=s, stop=p), SP, 1)
                res_mark["sqt_rd%d" % rr] = [mon]
                return mon

            for q in range(4):
                p_wi = P_WIF + 2 * q
                p_wf = P_WIF + 2 * q + 1
                sl_i = panel_slot(p_wi)
                sl_f = panel_slot(p_wf)
                for dt2 in range(4):
                    d = q * 4 + dt2
                    bii = 2 * (dt2 % 2)
                    bfi = bii + 1
                    bi = "fm%d" % bii
                    bf_ = "fm%d" % bfi
                    if d >= 3:
                        sq_sq(d - 3)
                    K.wait("tensor", xn_all)
                    bank_write(bi)
                    for kc in range(KC):
                        last = kc == KC - 1
                        mi = K.emit("tensor", lambda e, o=fm[bii][:],
                                    l=sl_i[:, kc * 512 + dt2 * 128: kc * 512 + (dt2 + 1) * 128],
                                    r=xn_sb[:, xnb + kc * TC: xnb + (kc + 1) * TC], s=(kc == 0), p=last:
                                    e.matmul(o, l, r, start=s, stop=p), SP if last else None, 1)
                    bank_st[bi]["write"] = mi
                    if d >= 3:
                        sq_mm(d - 3)
                    bank_write(bf_)
                    for kc in range(KC):
                        last = kc == KC - 1
                        mf = K.emit("tensor", lambda e, o=fm[bfi][:],
                                    l=sl_f[:, kc * 512 + dt2 * 128: kc * 512 + (dt2 + 1) * 128],
                                    r=xn_sb[:, xnb + kc * TC: xnb + (kc + 1) * TC], s=(kc == 0), p=last:
                                    e.matmul(o, l, r, start=s, stop=p), SP if last else None, 1)
                    bank_st[bf_]["write"] = mf
                    if dt2 == 3:
                        panel_done(p_wi, mi)
                        panel_done(p_wf, mf)
                    rr = d % 2
                    K.waits("scalar", res_mark.get("ifr%d" % rr, []))
                    bank_read("scalar", bi)
                    ma = K.emit("scalar", lambda e, o=tA[rr][:], i=fm[bii][:]:
                                e.activation(o, i, AF.Sigmoid))
                    bank_read("scalar", bf_)
                    mb = K.emit("scalar", lambda e, o=tB[rr][:], i=fm[bfi][:]:
                                e.activation(o, i, AF.Sigmoid, scale=-1.0))
                    mc = K.emit("scalar", lambda e, o=tC[rr][:], i=fm[bfi][:]:
                                e.activation(o, i, AF.Sigmoid))
                    bank_read_done(bf_, mc)
                    K.wait("vector", ma)
                    bank_read("vector", bi)
                    m1 = K.emit("vector", lambda e, o=tD[rr][:], a=fm[bii][:], b=tA[rr][:]:
                                e.tensor_mul(o, a, b))
                    bank_read_done(bi, m1)
                    K.wait("vector", mb)
                    m2 = K.emit("vector", lambda e, o=tD[rr][:], a=tD[rr][:], b=tB[rr][:]:
                                e.tensor_mul(o, a, b))
                    K.wait("vector", mc)
                    if prev_wo is not None:
                        K.wait("vector", prev_wo)
                    if debug:
                        K.waits("vector", res_mark.get("dbg_o_gate", []))
                    m3 = K.emit("vector", lambda e, o=o_sb[:, d * TC:(d + 1) * TC], g=tC[rr][:],
                                v=tD[rr][:], ini=carry_sb[:, d:d + 1]:
                                e.tensor_tensor_scan(o, g, v, ini, ALU.mult, ALU.add))
                    K.emit("vector", lambda e, o=carry_sb[:, d:d + 1],
                           i=o_sb[:, d * TC + TC - 1: d * TC + TC]: e.tensor_copy(o, i))
                    res_mark["ifr%d" % rr] = [m2, m3]
                    res_mark["tD_rd%d" % rr] = [m3]
                    res_mark["scan%d" % d] = m3

            if debug and c == 0:
                dbg_gate.append(dump(dbg_o, o_sb[:], [res_mark["scan%d" % dd] for dd in range(KC)]))
                res_mark["dbg_o_gate"] = [dbg_gate[-1]]

            def g_mm(d):
                q, dt2 = d // 4, d % 4
                sl_g = panel_slot(P_WG + q)
                bgi = d % 4
                bg = "fm%d" % bgi
                bank_write(bg)
                for kc in range(KC):
                    last = kc == KC - 1
                    mg = K.emit("tensor", lambda e, o=fm[bgi][:],
                                l=sl_g[:, kc * 512 + dt2 * 128: kc * 512 + (dt2 + 1) * 128],
                                r=xn_sb[:, xnb + kc * TC: xnb + (kc + 1) * TC], s=(kc == 0), p=last:
                                e.matmul(o, l, r, start=s, stop=p), SP if last else None, 1)
                bank_st[bg]["write"] = mg
                if dt2 == 3:
                    panel_done(P_WG + q, mg)

            def g_gate(d):
                bgi = d % 4
                bg = "fm%d" % bgi
                rr = d % 2
                K.waits("scalar", res_mark.get("sgr%d" % rr, []))
                bank_read("scalar", bg)
                msg = K.emit("scalar", lambda e, o=sg_sb[rr][:], i=fm[bgi][:]:
                             e.activation(o, i, AF.Silu))
                bank_read_done(bg, msg)
                K.wait("vector", msg)
                K.wait("vector", m_gn)
                bank_read("vector", "pt0")
                K.waits("vector", res_mark.get("tE_rd%d" % rr, []))
                mt = K.emit("vector", lambda e, o=tE32[:], i=o_sb[:, d * TC:(d + 1) * TC],
                            s_=gn_sb[:, d:d + 1], b=pt[0][:, 0:512]:
                            e.scalar_tensor_tensor(o, i, s_, b, ALU.mult, ALU.mult))
                bank_read_done("pt0", mt)
                K.waits("vector", dbg_gate)
                mog = K.emit("vector", lambda e, o=o_sb[:, d * TC:(d + 1) * TC], a=tE32[:],
                             b=sg_sb[rr][:]: e.tensor_mul(o, a, b))
                res_mark["sgr%d" % rr] = [mog]
                res_mark["tE_rd%d" % rr] = [mog]
                res_mark["og%d" % d] = mog

            sq_sq(KC - 3)
            g_mm(0)
            sq_mm(KC - 3)
            sq_sq(KC - 2)
            g_mm(1)
            sq_mm(KC - 2)
            sq_sq(KC - 1)
            g_mm(2)
            mon15 = sq_mm(KC - 1)
            g_mm(3)
            bank_st["pt0"]["write"] = mon15

            bank_read("vector", "pt0")
            K.emit("vector", lambda e: e.tensor_scalar(rsq_sb[:], pt[0][0:1, 0:512], 1.0 / H, EPS,
                                                       ALU.mult, ALU.add))
            m_rc = K.emit("vector", lambda e: e.tensor_copy(rsq_sb[:], rsq_sb[:]))
            K.wait("scalar", m_rc)
            m_rs = K.emit("scalar", lambda e: e.activation(rsq_sb[:], rsq_sb[:], AF.Sqrt))
            K.wait("vector", m_rs)
            m_rq = K.emit("vector", lambda e: e.reciprocal(rsq_sb[:], rsq_sb[:]))
            bank_read_done("pt0", m_rq)
            bank_write("pt0")
            K.wait("tensor", m_rq)
            m_rb = K.emit("tensor", lambda e: e.matmul(pt[0][:, 0:512], oner_sb[:],
                                                       rsq_sb[:], start=True, stop=True), SP, 1)
            bank_st["pt0"]["write"] = m_rb
            for d in range(4):
                g_gate(d)
            for d in range(4, KC):
                g_mm(d)
                g_gate(d)

            og_all = res_mark["og15"]
            if debug and c == 0:
                dbg_gate.append(dump(dbg_og, o_sb[:], [og_all]))
                dbg_gate.append(dump(dbg_rsq, rsq_sb[:], [m_rq]))

            if c >= 1:
                K.wait("scalar", res_mark["out_store"])
                K.wait("scalar", (SXA, SXA.n))
                m_oax = K.emit("scalar", lambda e, d=oa_sb[:].rearrange("p (n f) -> p n f", n=NT),
                               s=x_d[c * TC:(c + 1) * TC, :].rearrange("(n p) f -> p n f", p=128):
                               e.dma_start(d, s), SXA, 16)
                res_mark["oa_x"] = m_oax

            for q in range(4):
                p_g = P_SHG + 2 * q
                p_u = P_SHG + 2 * q + 1
                sl_g = panel_slot(p_g)
                sl_u = panel_slot(p_u)
                for ft2 in range(4):
                    fs = q * 4 + ft2
                    bgi = 2 * (ft2 % 2)
                    bui = bgi + 1
                    bg = "fm%d" % bgi
                    bu = "fm%d" % bui
                    bank_write(bg)
                    for kc in range(KC):
                        last = kc == KC - 1
                        mg = K.emit("tensor", lambda e, o=fm[bgi][:],
                                    l=sl_g[:, kc * 512 + ft2 * 128: kc * 512 + (ft2 + 1) * 128],
                                    r=xn_sb[:, xnb + kc * TC: xnb + (kc + 1) * TC], s=(kc == 0), p=last:
                                    e.matmul(o, l, r, start=s, stop=p), SP if last else None, 1)
                    bank_st[bg]["write"] = mg
                    bank_write(bu)
                    for kc in range(KC):
                        last = kc == KC - 1
                        mu = K.emit("tensor", lambda e, o=fm[bui][:],
                                    l=sl_u[:, kc * 512 + ft2 * 128: kc * 512 + (ft2 + 1) * 128],
                                    r=xn_sb[:, xnb + kc * TC: xnb + (kc + 1) * TC], s=(kc == 0), p=last:
                                    e.matmul(o, l, r, start=s, stop=p), SP if last else None, 1)
                    bank_st[bu]["write"] = mu
                    if ft2 == 3:
                        panel_done(p_g, mg)
                        panel_done(p_u, mu)
                    rr = fs % 2
                    K.waits("scalar", res_mark.get("tD_rd%d" % rr, []))
                    bank_read("scalar", bg)
                    m1 = K.emit("scalar", lambda e, o=tD[rr][:], i=fm[bgi][:]:
                                e.activation(o, i, AF.Silu))
                    bank_read_done(bg, m1)
                    K.wait("vector", m1)
                    bank_read("vector", bu)
                    K.waits("vector", res_mark.get("act_wr", []))
                    K.waits("vector", dbg_gate)
                    m2 = K.emit("vector", lambda e, o=act_sb[:, fs * TC:(fs + 1) * TC], a=tD[rr][:],
                                b=fm[bui][:]: e.tensor_mul(o, a, b))
                    bank_read_done(bu, m2)
                    res_mark["tD_rd%d" % rr] = [m2]
                    res_mark["acts%d" % fs] = m2
            res_mark["act_wr"] = []

            K.wait("vector", res_mark["oa_x"])
            for q in range(4):
                p_wo = P_WO + q
                sl_o = panel_slot(p_wo)
                for tt in range(NT):
                    pbi = ptn[0] % 2
                    pb = "pt%d" % pbi
                    ptn[0] += 1
                    bank_write(pb)
                    K.wait("tensor", og_all)
                    for kc in range(KC):
                        last = kc == KC - 1
                        mwo = K.emit("tensor", lambda e, o=pt[pbi][:, 0:512],
                                     l=o_sb[:, kc * TC + tt * 128: kc * TC + (tt + 1) * 128],
                                     r=sl_o[:, kc * 512:(kc + 1) * 512], s=(kc == 0), p=last:
                                     e.matmul(o, l, r, start=s, stop=p), SP if last else None, 1)
                    bank_st[pb]["write"] = mwo
                    if tt == NT - 1:
                        panel_done(p_wo, mwo)
                    bank_read("vector", pb)
                    oa_sl = oa_sb[:, tt * H + q * 512: tt * H + (q + 1) * 512]
                    mev = K.emit("vector", lambda e, o=oa_sl, a=oa_sl, b=pt[pbi][:, 0:512]:
                                 e.tensor_add(o, a, b))
                    bank_read_done(pb, mev)
                    oa_reads.append(mev)
            res_mark["wo_done"] = (SP, SP.n)
            if debug and c == 0:
                dbg_gate.append(dump(dbg_oa1, oa_sb[:], list(oa_reads)))
                dbg_gate.append(dump(dbg_act, act_sb[:], [res_mark["acts15"]]))

            actsh_all = res_mark["acts15"]
            for q in range(4):
                p_d = P_SHD + q
                sl_d = panel_slot(p_d)
                for tt in range(NT):
                    pbi = ptn[0] % 2
                    pb = "pt%d" % pbi
                    ptn[0] += 1
                    bank_write(pb)
                    K.wait("tensor", actsh_all)
                    for kc in range(KC):
                        last = kc == KC - 1
                        md = K.emit("tensor", lambda e, o=pt[pbi][:, 0:512],
                                    l=act_sb[:, kc * TC + tt * 128: kc * TC + (tt + 1) * 128],
                                    r=sl_d[:, kc * 512:(kc + 1) * 512], s=(kc == 0), p=last:
                                    e.matmul(o, l, r, start=s, stop=p), SP if last else None, 1)
                    bank_st[pb]["write"] = md
                    if tt == NT - 1:
                        panel_done(p_d, md)
                    bank_read("vector", pb)
                    oa_sl = oa_sb[:, tt * H + q * 512: tt * H + (q + 1) * 512]
                    K.waits("vector", dbg_gate)
                    mev = K.emit("vector", lambda e, o=oa_sl, a=oa_sl, b=pt[pbi][:, 0:512]:
                                 e.tensor_add(o, a, b))
                    bank_read_done(pb, mev)
                    oa_reads.append(mev)
            res_mark["exp_act_rd0"] = [(SP, SP.n)]
            res_mark["exp_act_rd1"] = [(SP, SP.n)]
            if debug and c == 0:
                dbg_gate.append(dump(dbg_oa2, oa_sb[:], list(oa_reads)))

            def emit_gu(n):
                half = n % 2
                abase = half * 8 * TC
                K.waits("vector", res_mark.get("exp_act_rd%d" % half, []))
                amarks = []
                for h2 in range(2):
                    p_g = P_GU[n] + 2 * h2
                    p_u = P_GU[n] + 2 * h2 + 1
                    sl_g = panel_slot(p_g)
                    sl_u = panel_slot(p_u)
                    for ft2 in range(4):
                        fc = h2 * 4 + ft2
                        bgi = 2 * (ft2 % 2)
                        bui = bgi + 1
                        bg = "fm%d" % bgi
                        bu = "fm%d" % bui
                        bank_write(bg)
                        for kc in range(KC):
                            last = kc == KC - 1
                            mg = K.emit("tensor", lambda e, o=fm[bgi][:],
                                        l=sl_g[:, kc * 512 + ft2 * 128: kc * 512 + (ft2 + 1) * 128],
                                        r=xn_sb[:, xnb + kc * TC: xnb + (kc + 1) * TC],
                                        s=(kc == 0), p=last:
                                        e.matmul(o, l, r, start=s, stop=p), SP if last else None, 1)
                        bank_st[bg]["write"] = mg
                        bank_write(bu)
                        for kc in range(KC):
                            last = kc == KC - 1
                            mu = K.emit("tensor", lambda e, o=fm[bui][:],
                                        l=sl_u[:, kc * 512 + ft2 * 128: kc * 512 + (ft2 + 1) * 128],
                                        r=xn_sb[:, xnb + kc * TC: xnb + (kc + 1) * TC],
                                        s=(kc == 0), p=last:
                                        e.matmul(o, l, r, start=s, stop=p), SP if last else None, 1)
                        bank_st[bu]["write"] = mu
                        if ft2 == 3:
                            panel_done(p_g, mg)
                            panel_done(p_u, mu)
                        rr = fc % 2
                        K.waits("scalar", res_mark.get("tD_rd%d" % rr, []))
                        bank_read("scalar", bg)
                        m1 = K.emit("scalar", lambda e, o=tD[rr][:], i=fm[bgi][:]:
                                    e.activation(o, i, AF.Silu))
                        bank_read_done(bg, m1)
                        K.wait("vector", m1)
                        bank_read("vector", bu)
                        m2 = K.emit("vector", lambda e,
                                    o=act_sb[:, abase + fc * TC: abase + (fc + 1) * TC],
                                    a=tD[rr][:], b=fm[bui][:]: e.tensor_mul(o, a, b))
                        bank_read_done(bu, m2)
                        res_mark["tD_rd%d" % rr] = [m2]
                        amarks.append(m2)
                res_mark["exp_act%d" % n] = amarks[-1]

            def emit_down(n):
                half = n % 2
                abase = half * 8 * TC
                act_all = res_mark["exp_act%d" % n]
                dreads = []
                for dp in range(2):
                    p_d = P_D[n] + dp
                    sl_d = panel_slot(p_d)
                    for tt in range(NT):
                        pbi = ptn[0] % 2
                        pb = "pt%d" % pbi
                        ptn[0] += 1
                        bank_write(pb)
                        K.wait("tensor", act_all)
                        for fc in range(8):
                            for jh in range(2):
                                last = fc == 7 and jh == 1
                                md = K.emit("tensor", lambda e,
                                            o=pt[pbi][:, jh * 512:(jh + 1) * 512],
                                            l=act_sb[:, abase + fc * TC + tt * 128: abase + fc * TC + (tt + 1) * 128],
                                            r=sl_d[:, fc * 1024 + jh * 512: fc * 1024 + (jh + 1) * 512],
                                            s=(fc == 0), p=(fc == 7):
                                            e.matmul(o, l, r, start=s, stop=p), SP if last else None, 1)
                        bank_st[pb]["write"] = md
                        dreads.append(md)
                        if tt == NT - 1:
                            panel_done(p_d, md)
                        bank_read("vector", pb)
                        oa_sl = oa_sb[:, tt * H + dp * 1024: tt * H + (dp + 1) * 1024]
                        w_ap = wexp_sb[:, par * 64 + tt * NE + n: par * 64 + tt * NE + n + 1]
                        K.waits("vector", dbg_gate)
                        mev = K.emit("vector", lambda e, o=oa_sl, a=pt[pbi][:], w=w_ap, b=oa_sl:
                                     e.scalar_tensor_tensor(o, a, w, b, ALU.mult, ALU.add))
                        bank_read_done(pb, mev)
                        oa_reads.append(mev)
                res_mark["exp_act_rd%d" % half] = [dreads[-1]]

            for n in range(NE):
                if 5 <= n <= 8 and c + 1 < NCH:
                    head_dma(c + 1, n - 5)
                if 6 <= n <= 9 and c + 1 < NCH:
                    head_ln(c + 1, n - 6)
                emit_gu(n)
                if n >= 1:
                    emit_down(n - 1)
                if 6 <= n <= 9 and c + 1 < NCH:
                    head_tr(c + 1, n - 6)
            emit_down(NE - 1)
            res_mark["act_wr"] = [(SP, SP.n)]
            res_mark["xn_rd%d" % par] = [(SP, SP.n)]

            K.waits("sync", oa_reads)
            K.wait("sync", (SOUT, SOUT.n))
            m_out = K.emit("sync", lambda e,
                           d=out_d[row0:row0 + TC, :].rearrange("(n p) f -> p n f", p=128),
                           s=oa_sb[:].rearrange("p (n f) -> p n f", n=NT): e.dma_start(d, s),
                           SOUT, 16)
            res_mark["out_store"] = m_out

        K.wait("sync", (SOUT, SOUT.n))
        K.wait("sync", (SX, SX.n))
        K.wait("sync", (SXA, SXA.n))
        K.wait("sync", (SPR, SPR.n))
        K.wait("sync", (SWR, SWR.n))
        K.wait("sync", (SXT, SXT.n))
        for i in range(4):
            K.wait("sync", (SLg[i], SLg[i].n))
            K.wait("sync", (SLs[i], SLs[i].n))
            K.wait("sync", (SW[i], SW[i].n))
        if debug:
            K.wait("sync", (SDB, SDB.n))

        for sem_, cnt_ in K.all_waits:
            assert cnt_ <= sem_.n, f"wait target {cnt_} exceeds final count {sem_.n} -> would hang"

        with nc.Block() as block:
            K.flush(block)

    return nc


_NC_CACHE = [None]


def kernel(**inp):
    x = np.ascontiguousarray(np.asarray(inp["L0_attn_hidden_states"], dtype=np.float32))
    t = lambda a: np.ascontiguousarray(np.asarray(a, dtype=np.float32).T)
    wi_t = t(inp["L0_attn_Wi"])
    wf_t = t(inp["L0_attn_Wf"])
    wg_t = t(inp["L0_attn_Wg"])
    wo_t = t(inp["L0_attn_Wo"])
    wr_t = t(inp["L0_ffn_gate_weight"])
    gnorm = np.asarray(inp["L0_attn_gnorm_w"], dtype=np.float32)
    gnorm_fm = np.ascontiguousarray(gnorm.reshape(KC, 128).T)
    bias_r = np.ascontiguousarray(np.asarray(inp["L0_ffn_e_score_correction_bias"], dtype=np.float32))
    tr3 = lambda a: np.ascontiguousarray(np.asarray(a, dtype=np.float32).transpose(0, 2, 1))
    gw_t = tr3(inp["L0_ffn_gate_w"])
    uw_t = tr3(inp["L0_ffn_up_w"])
    dw_t = tr3(inp["L0_ffn_down_w"])
    shg_t = t(inp["L0_ffn_sh_gate_w"])
    shu_t = t(inp["L0_ffn_sh_up_w"])
    shd_t = t(inp["L0_ffn_sh_down_w"])

    if _NC_CACHE[0] is None:
        _NC_CACHE[0] = build_core()
    nc = _NC_CACHE[0]

    common = dict(wi_t=wi_t, wf_t=wf_t, wg_t=wg_t, wo_t=wo_t, wr_t=wr_t,
                  gnorm_fm=gnorm_fm, bias_r=bias_r, gw_t=gw_t, uw_t=uw_t,
                  dw_t=dw_t, shg_t=shg_t, shu_t=shu_t, shd_t=shd_t)
    in_maps = [dict(x=np.ascontiguousarray(x[b]), **common) for b in range(8)]
    res = run_bass_kernel_spmd(nc, in_maps, core_ids=list(range(8)))
    return np.stack([res.results[b]["out"] for b in range(8)], axis=0)
